# version 01
# baseline (speedup 1.0000x reference)
import numpy as np
import concourse.bass as bass
import concourse.mybir as mybir
from concourse.bass_utils import run_bass_kernel_spmd

F32 = mybir.dt.float32
BF16 = mybir.dt.bfloat16
U8 = mybir.dt.uint8
AF = mybir.ActivationFunctionType
ALU = mybir.AluOpType
AX = mybir.AxisListType

T = 2048
D = 2048
H = 16
HID = 5632
NT = T // 128
ND = D // 128
NH = HID // 128
ISQ = 1.0 / float(np.sqrt(2048.0))
EPS = 1e-5
GROUPS = [(0, 6), (6, 6), (12, 4)]

DEBUG_TAPS = ()



class Buf:
    __slots__ = ("name", "writer", "readers")

    def __init__(self, name):
        self.name = name
        self.writer = None
        self.readers = []


class Op:
    __slots__ = ("eng", "emit", "deps", "needs_inc", "incval", "seq",
                 "small", "is_dma", "dma_idx", "note")

    def __init__(self, eng, emit, small, is_dma, note):
        self.eng = eng
        self.emit = emit
        self.deps = []
        self.needs_inc = False
        self.incval = None
        self.seq = -1
        self.small = small
        self.is_dma = is_dma
        self.dma_idx = None
        self.note = note


class Builder:

    def __init__(self):
        self.ops = []

    def op(self, eng, emit, reads=(), writes=(), small=True, note=""):
        o = Op(eng, emit, small, False, note)
        for b in reads:
            if b.writer is not None:
                o.deps.append(b.writer)
            b.readers.append(o)
        for b in writes:
            o.deps.extend(b.readers)
            if b.writer is not None:
                o.deps.append(b.writer)
            b.readers = []
            b.writer = o
        o.seq = len(self.ops)
        self.ops.append(o)
        return o

    def dma(self, eng, emit, reads=(), writes=(), note=""):
        o = self.op(eng, emit, reads, writes, small=False, note=note)
        o.is_dma = True
        return o

    @staticmethod
    def merge(new_buf, old_bufs):
        for ob in old_bufs:
            if ob.writer is not None and ob.writer not in new_buf.readers:
                new_buf.readers.append(ob.writer)
            for r in ob.readers:
                if r not in new_buf.readers:
                    new_buf.readers.append(r)

    def finalize(self):
        for o in self.ops:
            o.deps = [d for d in o.deps if d is not o]
            for d in o.deps:
                if d.eng != o.eng and not d.is_dma:
                    d.needs_inc = True
        cnt = {}
        dma_cnt = {}
        for o in self.ops:
            if o.is_dma:
                dma_cnt[o.eng] = dma_cnt.get(o.eng, 0) + 1
                o.dma_idx = dma_cnt[o.eng]
            elif o.needs_inc:
                cnt[o.eng] = cnt.get(o.eng, 0) + 1
                o.incval = cnt[o.eng]

    def emit_engine(self, eng_name, eng, sems, dma_sems):
        waited = {}
        last_dp = None
        last_dp2 = None
        for o in self.ops:
            if o.eng != eng_name:
                continue
            need_nop = 0
            for d in o.deps:
                if d.eng == eng_name:
                    if d.small and not d.is_dma:
                        if d is last_dp:
                            need_nop = max(need_nop, 2)
                        elif d is last_dp2:
                            need_nop = max(need_nop, 1)
                    continue
                if d.is_dma:
                    sem = dma_sems[d.eng]
                    val = 16 * d.dma_idx
                else:
                    if d.incval is None:
                        raise RuntimeError(f"dep without inc: {d.note} <- {o.note}")
                    sem = sems[d.eng]
                    val = d.incval
                key = id(sem)
                if waited.get(key, 0) < val:
                    eng.wait_ge(sem, val)
                    waited[key] = val
            for _ in range(need_nop):
                if hasattr(eng, "engine_nop"):
                    eng.engine_nop()
                else:
                    eng.drain()
            if need_nop:
                last_dp = None
                last_dp2 = None
            ins = o.emit(eng)
            if o.is_dma:
                ins.then_inc(dma_sems[o.eng], 16)
            elif o.needs_inc:
                ins.then_inc(sems[o.eng], 1)
            last_dp2 = last_dp
            last_dp = o



def _u8(q):
    return np.ascontiguousarray(np.asarray(q).view(np.uint8)
                                .reshape(np.asarray(q).shape + (4,))[..., 0])


def pack_inputs(Wq, Wk, Wv, Wo, wq_gate, s_gate, wq_up, s_up, wq_down, s_down):
    import ml_dtypes
    out = {}
    aw = np.stack([np.asarray(w, dtype=np.float32) for w in (Wq, Wk, Wv, Wo)])
    aw = aw.reshape(4, ND, 128, ND, 128).transpose(0, 3, 2, 1, 4)
    out["attw"] = np.ascontiguousarray(aw.reshape(4, ND, 128, D))

    def pack_gu(q):
        w8 = _u8(q)
        w8 = w8.reshape(NH, 128, ND, 128).transpose(0, 3, 2, 1)
        return w8.reshape(NH, 128, D)

    g = pack_gu(wq_gate)
    u = pack_gu(wq_up)
    out["gu8"] = np.ascontiguousarray(np.concatenate([g, u], axis=2))
    d8 = _u8(wq_down)
    out["dn8"] = np.ascontiguousarray(d8.reshape(D, NH, 128).transpose(1, 2, 0))
    out["sg"] = np.ascontiguousarray(
        np.asarray(s_gate, dtype=np.float32).reshape(NH, 128).T)
    out["su"] = np.ascontiguousarray(
        np.asarray(s_up, dtype=np.float32).reshape(NH, 128).T)
    out["sdbc"] = np.ascontiguousarray(
        np.broadcast_to(np.asarray(s_down, dtype=np.float32)[None, :], (128, D)))
    out["ident"] = np.eye(128, dtype=ml_dtypes.bfloat16)
    m = np.zeros((128, 128), dtype=np.float32)
    a = np.arange(128)
    m[a[:, None] >= a[None, :]] = 1.0
    out["mask"] = m
    return out



def build_program(taps=()):
    nc = bass.Bass("TRN2", target_bir_lowering=False, debug=False)
    taps = set(taps)

    x_d = nc.dram_tensor("x", [T, D], F32, kind="ExternalInput").ap()
    attw_d = nc.dram_tensor("attw", [4, ND, 128, D], F32, kind="ExternalInput").ap()
    gu8_d = nc.dram_tensor("gu8", [NH, 128, 4096], U8, kind="ExternalInput").ap()
    dn8_d = nc.dram_tensor("dn8", [NH, 128, D], U8, kind="ExternalInput").ap()
    sg_d = nc.dram_tensor("sg", [128, NH], F32, kind="ExternalInput").ap()
    su_d = nc.dram_tensor("su", [128, NH], F32, kind="ExternalInput").ap()
    sdbc_d = nc.dram_tensor("sdbc", [128, D], F32, kind="ExternalInput").ap()
    ident_d = nc.dram_tensor("ident", [128, 128], BF16, kind="ExternalInput").ap()
    mask_d = nc.dram_tensor("mask", [128, 128], F32, kind="ExternalInput").ap()
    y_d = nc.dram_tensor("y", [T, D], F32, kind="ExternalOutput").ap()

    tap_d = {}

    def tap_dram(name, shape, dtype):
        if name not in tap_d:
            tap_d[name] = nc.dram_tensor("tap_" + name, shape, dtype,
                                         kind="ExternalOutput").ap()
        return tap_d[name]

    arenaA = nc.alloc_sbuf_tensor("arenaA", [128, 67584], U8)
    arenaB = nc.alloc_sbuf_tensor("arenaB", [128, 65536], U8)
    arenaD = nc.alloc_sbuf_tensor("arenaD", [128, 36864], U8)
    arenaE = nc.alloc_sbuf_tensor("arenaE", [128, 24576], U8)
    arenaF = nc.alloc_sbuf_tensor("arenaF", [128, 8192], U8)
    sqscr = nc.alloc_sbuf_tensor("sqscr", [128, 1024], BF16)
    ident = nc.alloc_sbuf_tensor("ident_s", [128, 128], BF16)
    mask = nc.alloc_sbuf_tensor("mask_s", [128, 128], F32)
    sg_s = nc.alloc_sbuf_tensor("sg_s", [128, NH], F32)
    su_s = nc.alloc_sbuf_tensor("su_s", [128, NH], F32)
    ones1 = nc.alloc_sbuf_tensor("ones1", [128, 128], BF16)

    def tiny(name, w=1, dt=F32):
        return nc.alloc_sbuf_tensor(name, [128, w], dt)

    m0 = tiny("m0"); mh2 = tiny("mh2"); bias = tiny("bias")
    expnegm = tiny("expnegm"); zacc1 = tiny("zacc1"); zacc2 = tiny("zacc2")
    zfin = tiny("zfin"); rz = tiny("rz")
    frcols = tiny("frcols", 4, BF16)
    sfutc = tiny("sfutc", 16)
    sfuts = tiny("sfuts", 16, BF16)
    sfr = nc.alloc_sbuf_tensor("sfr", [1, T], BF16)
    frr = nc.alloc_sbuf_tensor("frr", [1, 512], BF16)
    ssum = tiny("ssum", 2); ssq = tiny("ssq", 2); varv = tiny("varv", 2)
    nmean = tiny("nmean", 2); istd = tiny("istd", 2)
    nt1 = tiny("nt1", 2); nt2 = tiny("nt2", 2)
    gusum = tiny("gusum", 2); gsc = tiny("gsc", 2)
    sufacc = tiny("sufacc", 1)

    def view(arena, byte_off, shape, dt):
        n = int(np.prod(shape[1:])) * mybir.dt.size(dt)
        ap = arena.ap()[:, byte_off:byte_off + n].bitcast(dt)
        if len(shape) == 2:
            return ap
        return ap.rearrange("p (a b) -> p a b", a=shape[1])

    y1fm = view(arenaA, 0, [128, ND, T], BF16)
    otm = view(arenaA, 0, [128, NT, D], BF16)
    gu = view(arenaA, 0, [128, NH, 768], BF16)
    att = view(arenaB, 0, [128, H, T], BF16)
    x1 = view(arenaB, 0, [128, NT, D], BF16)
    xin = [view(arenaD, 0, [128, D], F32), view(arenaD, 8192, [128, D], F32)]
    q_sb = view(arenaD, 0, [128, T], BF16)
    k_sb = view(arenaD, 4096, [128, T], BF16)
    vfm = view(arenaD, 8192, [128, T], BF16)
    vtm = view(arenaD, 12288, [128, ND, 128], BF16)
    p_sb = view(arenaD, 16384, [128, T], BF16)
    pt = view(arenaD, 20480, [128, 16, 512], BF16)
    ofm_t = view(arenaD, 16384, [128, T], BF16)
    y2fm = view(arenaD, 0, [128, ND, 768], BF16)
    sdbc = view(arenaD, 24576, [128, D], F32)
    wq_b = view(arenaE, 0, [128, ND, 128], BF16)
    wk_b = view(arenaE, 4096, [128, ND, 128], BF16)
    wv_b = view(arenaE, 8192, [128, ND, 128], BF16)
    wo_b = [view(arenaE, 0, [128, ND, 128], BF16),
            view(arenaE, 4096, [128, ND, 128], BF16)]
    xinC = [view(arenaE, 8192, [128, D], F32), view(arenaE, 16384, [128, D], F32)]
    guw_b = [view(arenaE, 0, [128, 32, 128], BF16),
             view(arenaE, 8192, [128, 32, 128], BF16)]
    wd_b = [view(arenaE, 16384, [128, D], BF16), view(arenaE, 20480, [128, D], BF16)]
    ftile_bf = [view(arenaF, 0, [128, D], BF16), view(arenaF, 4096, [128, D], BF16)]
    fhalf_f32 = [view(arenaF, 0, [128, 1024], F32), view(arenaF, 4096, [128, 1024], F32)]

    PAA = nc.alloc_psum_tensor("PAA", [128, 1024], F32)
    PAB = nc.alloc_psum_tensor("PAB", [128, 1024], F32)
    PAV = nc.alloc_psum_tensor("PAV", [128, 512], F32)
    PTA = nc.alloc_psum_tensor("PTA", [128, 512], F32)
    PTB = nc.alloc_psum_tensor("PTB", [128, 512], F32)
    PC = nc.alloc_psum_tensor("PC", [128, 512], F32)

    B = Builder()
    bufs = {}

    def buf(name):
        if name not in bufs:
            bufs[name] = Buf(name)
        return bufs[name]

    def mm(out_ap, lhsT_ap, rhs_ap, start, stop, reads, writes, note=""):
        B.op("pe", lambda e, o=out_ap, l=lhsT_ap, r=rhs_ap, st=start, sp=stop:
             e.matmul(o, l, r, start=st, stop=sp),
             reads=reads, writes=writes, small=False, note=note)

    def vop(fn, reads, writes, small=True, note=""):
        B.op("dve", fn, reads=reads, writes=writes, small=small, note=note)

    def aop(fn, reads, writes, small=True, note=""):
        B.op("act", fn, reads=reads, writes=writes, small=small, note=note)

    def sdma(dst, src, reads, writes, note=""):
        B.dma("sp", lambda e, d=dst, s=src: e.dma_start(d, s),
              reads=reads, writes=writes, note=note)

    def pdma(dst, src, reads, writes, note=""):
        B.dma("pl", lambda e, d=dst, s=src: e.dma_start(d, s),
              reads=reads, writes=writes, note=note)

    def ln_stats(pieces, srcbufs, par, n_feat, note):
        sb = buf("lnstat" + str(par))
        first = True
        for ap_, w_ in pieces:
            if first:
                vop(lambda e, a=ap_: e.reduce_sum(ssum[:, par:par + 1], a, axis=AX.X),
                    reads=list(srcbufs), writes=[sb], small=True, note=note + "sum")
                first = False
            else:
                vop(lambda e, a=ap_: e.reduce_sum(nt1[:, par:par + 1], a, axis=AX.X),
                    reads=list(srcbufs), writes=[sb], small=True, note=note + "sum2")
                vop(lambda e: e.tensor_add(ssum[:, par:par + 1],
                                           ssum[:, par:par + 1], nt1[:, par:par + 1]),
                    reads=[sb], writes=[sb], note=note + "sumadd")
        subpieces = []
        for ap_, w_ in pieces:
            for a0 in range(0, w_, 1024):
                subpieces.append((ap_[:, a0:min(a0 + 1024, w_)], min(1024, w_ - a0)))
        firstq = True
        for ap_, w_ in subpieces:
            za = ssq if firstq else nt2
            aop(lambda e, a=ap_, w=w_, z=za: e.activation(
                sqscr[:, 0:w], a, AF.Square, accum_out=z[:, par:par + 1]),
                reads=list(srcbufs), writes=[sb, buf("sqscr")], small=False,
                note=note + "sq")
            B.op("act", lambda e: e.drain(), reads=[sb], writes=[sb],
                 note=note + "sqdr")
            if not firstq:
                vop(lambda e: e.tensor_add(ssq[:, par:par + 1],
                                           ssq[:, par:par + 1], nt2[:, par:par + 1]),
                    reads=[sb], writes=[sb], note=note + "sqadd")
            firstq = False
        inv_n = 1.0 / float(n_feat)
        vop(lambda e: e.tensor_scalar(nt1[:, par:par + 1], ssum[:, par:par + 1],
                                      inv_n * inv_n, ssum[:, par:par + 1],
                                      ALU.mult, ALU.mult),
            reads=[sb], writes=[sb], note=note + "msq")
        vop(lambda e: e.tensor_scalar(varv[:, par:par + 1], ssq[:, par:par + 1],
                                      inv_n, None, ALU.mult),
            reads=[sb], writes=[sb], note=note + "var0")
        vop(lambda e: e.tensor_sub(varv[:, par:par + 1], varv[:, par:par + 1],
                                   nt1[:, par:par + 1]),
            reads=[sb], writes=[sb], note=note + "var1")
        vop(lambda e: e.tensor_scalar(varv[:, par:par + 1], varv[:, par:par + 1],
                                      1.0, EPS, ALU.mult, ALU.add),
            reads=[sb], writes=[sb], note=note + "var2")
        vop(lambda e: e.tensor_scalar(nmean[:, par:par + 1], ssum[:, par:par + 1],
                                      -inv_n, None, ALU.mult),
            reads=[sb], writes=[sb], note=note + "nmean")
        aop(lambda e: e.activation(istd[:, par:par + 1], varv[:, par:par + 1],
                                   AF.Sqrt),
            reads=[sb], writes=[sb], note=note + "sqrt")
        vop(lambda e: e.reciprocal(istd[:, par:par + 1], istd[:, par:par + 1]),
            reads=[sb], writes=[sb], note=note + "recip")
        vop(lambda e: e.tensor_scalar(nt1[:, par:par + 1], istd[:, par:par + 1],
                                      1.0, istd[:, par:par + 1], ALU.mult, ALU.mult),
            reads=[sb], writes=[sb], note=note + "n0")
        vop(lambda e: e.tensor_scalar(nt1[:, par:par + 1], nt1[:, par:par + 1],
                                      varv[:, par:par + 1], None, ALU.mult),
            reads=[sb], writes=[sb], note=note + "n1")
        vop(lambda e: e.tensor_scalar(nt1[:, par:par + 1], nt1[:, par:par + 1],
                                      -0.5, 1.5, ALU.mult, ALU.add),
            reads=[sb], writes=[sb], note=note + "n2")
        vop(lambda e: e.tensor_scalar(istd[:, par:par + 1], istd[:, par:par + 1],
                                      nt1[:, par:par + 1], None, ALU.mult),
            reads=[sb], writes=[sb], note=note + "n3")
        return sb

    def transpose_mm(psum_slice, data_slice, reads, writes, note):
        B.op("pe", lambda e, o=psum_slice, l=data_slice:
             e.matmul(o, l, ident[:], start=True, stop=True),
             reads=list(reads) + [buf("ident")], writes=writes, small=False,
             note=note)

    sdma(ident[:], ident_d, reads=[], writes=[buf("ident")], note="ld_ident")
    sdma(mask[:], mask_d, reads=[], writes=[buf("mask")], note="ld_mask")
    sdma(sg_s[:], sg_d, reads=[], writes=[buf("sg")], note="ld_sg")
    sdma(su_s[:], su_d, reads=[], writes=[buf("su")], note="ld_su")
    vop(lambda e: e.memset(ones1[:], 1.0), reads=[], writes=[buf("ones1")],
        note="ones1")

    def tile_to_fm(src_ap, src_buf, dst_view, dst_buf, tcols, tag):
        for g in range(4):
            pr, prb = (PTA, buf("PTA")) if g % 2 == 0 else (PTB, buf("PTB"))
            for k in range(4):
                dch = g * 4 + k
                transpose_mm(pr[:, k * 128:(k + 1) * 128],
                             src_ap[:, dch * 128:(dch + 1) * 128],
                             [src_buf], [prb], f"{tag}T{dch}")
            dst = dst_view[:, g * 4:(g + 1) * 4, tcols[0]:tcols[1]]
            src = pr[:].rearrange("p (a b) -> p a b", a=4)
            vop(lambda e, d=dst, s=src: e.tensor_copy(d, s),
                reads=[prb], writes=[dst_buf, prb], small=True,
                note=f"{tag}ev{g}")

    for t in range(NT):
        xb = buf(f"xin{t % 2}")
        sdma(xin[t % 2][:], x_d[t * 128:(t + 1) * 128, :],
             reads=[], writes=[xb], note=f"xin{t}")
        par = t % 2
        sb = ln_stats([(xin[par][:], D)], [xb], par, D, f"ln1t{t}")
        fb = buf(f"ftile{par}")
        vop(lambda e, p=par: e.tensor_scalar(
            ftile_bf[p][:], xin[p][:], nmean[:, p:p + 1], istd[:, p:p + 1],
            ALU.add, ALU.mult),
            reads=[sb, xb], writes=[fb], small=False, note=f"ln1n{t}")
        tile_to_fm(ftile_bf[par], fb, y1fm, buf("y1fm"),
                   (t * 128, (t + 1) * 128), f"y1_{t}")
        if t == 0 and "dbg0" in taps:
            dd = tap_dram("xin0", [128, D], F32)
            sdma(dd[:], xin[0][:], reads=[xb], writes=[], note="tap_xin0")
            dd = tap_dram("ftile0", [128, D], BF16)
            sdma(dd[:], ftile_bf[0][:], reads=[fb], writes=[], note="tap_ft0")
            dd = tap_dram("stat0", [128, 8], F32)
            for ci, vv in enumerate((ssum, ssq, varv, istd)):
                sdma(dd[:, ci * 2:(ci + 1) * 2], vv[:], reads=[sb], writes=[],
                     note=f"tap_st{ci}")
    if "y1fm" in taps:
        dd = tap_dram("y1fm", [128, ND * T], BF16)
        sdma(dd[:], y1fm.rearrange("p a b -> p (a b)"),
             reads=[buf("y1fm")], writes=[], note="tap_y1fm")

    B.merge(buf("q_sb"), [buf("xin0")])
    B.merge(buf("k_sb"), [buf("xin0")])
    B.merge(buf("vfm"), [buf("xin1")])
    B.merge(buf("vtm"), [buf("xin1")])

    def head_w_dma(h):
        for wi, (wb, nm) in enumerate(((wq_b, "wq"), (wk_b, "wk"), (wv_b, "wv"))):
            pdma(wb.rearrange("p a b -> p (a b)"), attw_d[wi, h],
                 reads=[], writes=[buf(nm)], note=f"w{nm}{h}")

    head_w_dma(0)

    for h in range(H):
        for (wbuf_ap, wname, outsb, outbuf) in (
                (wq_b, "wq", q_sb, "q_sb"), (wk_b, "wk", k_sb, "k_sb"),
                (wv_b, "wv", vfm, "vfm")):
            pb_a, pb_b = buf("PAA"), buf("PAB")
            for dch in range(ND):
                for nch in range(4):
                    ps = (PAA if nch < 2 else PAB)[:, (nch % 2) * 512:(nch % 2 + 1) * 512]
                    mm(ps, wbuf_ap[:, dch, :], y1fm[:, dch, nch * 512:(nch + 1) * 512],
                       dch == 0, dch == ND - 1,
                       reads=[buf(wname), buf("y1fm")],
                       writes=[pb_a if nch < 2 else pb_b],
                       note=f"{wname}mm{h}_{dch}_{nch}")
            ob = buf(outbuf)
            vop(lambda e, o=outsb: e.tensor_copy(o[:, 0:1024], PAA[:]),
                reads=[pb_a], writes=[ob, pb_a], small=False, note=f"{wname}evA{h}")
            vop(lambda e, o=outsb: e.tensor_copy(o[:, 1024:2048], PAB[:]),
                reads=[pb_b], writes=[ob, pb_b], small=False, note=f"{wname}evB{h}")
        if h + 1 < H:
            head_w_dma(h + 1)
        for g in range(4):
            pr, prb = (PTA, buf("PTA")) if g % 2 == 0 else (PTB, buf("PTB"))
            for k in range(4):
                j = g * 4 + k
                transpose_mm(pr[:, k * 128:(k + 1) * 128],
                             vfm[:, j * 128:(j + 1) * 128],
                             [buf("vfm")], [prb], f"vT{h}_{j}")
            dst = vtm[:, g * 4:(g + 1) * 4, :]
            vop(lambda e, d=dst, s=pr[:].rearrange("p (a b) -> p a b", a=4):
                e.tensor_copy(d, s),
                reads=[prb], writes=[buf("vtm"), prb], small=True,
                note=f"vev{h}_{g}")
        sfb = buf("sfut")
        for j in range(ND):
            vop(lambda e, j_=j: e.reduce_sum(
                sfutc[:, j_:j_ + 1], vfm[:, j_ * 128:(j_ + 1) * 128], axis=AX.X),
                reads=[buf("vfm")], writes=[sfb], note=f"cs{h}_{j}")
        vop(lambda e: e.memset(sfuts[:, 15:16], 0.0), reads=[], writes=[sfb],
            note=f"suf15_{h}")
        vop(lambda e: e.memset(sufacc[:], 0.0), reads=[], writes=[sfb],
            note=f"sufa_{h}")
        for i in range(14, -1, -1):
            vop(lambda e, i_=i: e.tensor_add(
                sufacc[:], sufacc[:], sfutc[:, i_ + 1:i_ + 2]),
                reads=[sfb], writes=[sfb], note=f"sufadd{h}_{i}")
            vop(lambda e, i_=i: e.tensor_copy(sfuts[:, i_:i_ + 1], sufacc[:]),
                reads=[sfb], writes=[sfb], note=f"sufcp{h}_{i}")
        pcb = buf("PC")
        for g4 in range(4):
            for k in range(4):
                i = g4 * 4 + k
                B.op("pe", lambda e, i_=i, k_=k: e.matmul(
                    PC[0:1, k_ * 128:(k_ + 1) * 128], sfuts[:, i_:i_ + 1],
                    ident[:], start=True, stop=True),
                    reads=[sfb, buf("ident")], writes=[pcb], small=False,
                    note=f"sfT{h}_{i}")
            vop(lambda e, g_=g4: e.tensor_copy(
                sfr[0:1, g_ * 512:(g_ + 1) * 512], PC[0:1, 0:512]),
                reads=[pcb], writes=[buf("sfr"), pcb], small=True,
                note=f"sfev{h}_{g4}")

        for qq in range(4):
            for il in range(4):
                i = qq * 4 + il
                W = (i + 1) * 128
                if i < 8:
                    halves = [((PAA, "PAA") if i % 2 == 0 else (PAB, "PAB"))
                              + (0, W)]
                else:
                    halves = [(PAA, "PAA", 0, 1024), (PAB, "PAB", 1024, W)]
                for (pp, pname, c0, c1) in halves:
                    hb = buf(pname)
                    for a in range(c0, c1, 512):
                        bw = min(512, c1 - a)
                        mm(pp[:, a - c0:a - c0 + bw],
                           q_sb[:, i * 128:(i + 1) * 128], k_sb[:, a:a + bw],
                           True, True, [buf("q_sb"), buf("k_sb")], [hb],
                           note=f"S{h}_{i}_{a}")
                dpp, dname, dc0, _ = halves[-1]
                dhb = buf(dname)
                doff = i * 128 - dc0
                vop(lambda e, p=dpp, o=doff: e.tensor_mul(
                    p[:, o:o + 128], p[:, o:o + 128], mask[:]),
                    reads=[buf("mask")], writes=[dhb], small=True,
                    note=f"mask{h}_{i}")
                smb = buf("smax")
                first = True
                for (pp, pname, c0, c1) in halves:
                    tgt = m0 if first else mh2
                    vop(lambda e, p=pp, w=c1 - c0, tg=tgt: e.reduce_max(
                        tg[:, 0:1], p[:, 0:w], axis=AX.X),
                        reads=[buf(pname)], writes=[smb], small=True,
                        note=f"max{h}_{i}")
                    first = False
                if len(halves) > 1:
                    vop(lambda e: e.tensor_max(m0[:], m0[:], mh2[:]),
                        reads=[smb], writes=[smb], note=f"maxc{h}_{i}")
                vop(lambda e: e.tensor_scalar_max(m0[:], m0[:], 0.0),
                    reads=[smb], writes=[smb], note=f"relu{h}_{i}")
                vop(lambda e: e.tensor_scalar(bias[:], m0[:], -ISQ, None, ALU.mult),
                    reads=[smb], writes=[smb], note=f"bias{h}_{i}")
                zb = buf("zacc")
                first = True
                for (pp, pname, c0, c1) in halves:
                    za = zacc1 if first else zacc2
                    aop(lambda e, p=pp, a=c0, w=c1 - c0, z=za: e.activation(
                        p_sb[:, a:a + w], p[:, 0:w], AF.Exp,
                        bias=bias[:], scale=ISQ, accum_out=z[:]),
                        reads=[smb, buf(pname)], writes=[buf("p_sb"), zb, buf(pname)],
                        small=False, note=f"exp{h}_{i}")
                    first = False
                aop(lambda e: e.activation(expnegm[:], bias[:], AF.Exp),
                    reads=[smb], writes=[zb], note=f"enm{h}_{i}")
                B.op("act", lambda e: e.drain(), reads=[zb], writes=[zb],
                     note=f"zdr{h}_{i}")
                nf = float((15 - i) * 128)
                if len(halves) > 1:
                    vop(lambda e: e.tensor_add(zacc1[:], zacc1[:], zacc2[:]),
                        reads=[zb], writes=[zb], note=f"zc{h}_{i}")
                if nf > 0:
                    vop(lambda e, c=nf: e.tensor_scalar(zfin[:], expnegm[:], c,
                                                        None, ALU.mult),
                        reads=[zb], writes=[zb], note=f"znf{h}_{i}")
                    vop(lambda e: e.tensor_add(zacc1[:], zacc1[:], zfin[:]),
                        reads=[zb], writes=[zb], note=f"zadd{h}_{i}")
                vop(lambda e: e.reciprocal(rz[:], zacc1[:]),
                    reads=[zb], writes=[zb], note=f"rz{h}_{i}")
                vop(lambda e, w=W: e.tensor_scalar(
                    p_sb[:, 0:w], p_sb[:, 0:w], rz[:, 0:1], None, ALU.mult),
                    reads=[zb], writes=[buf("p_sb")], small=False,
                    note=f"pscl{h}_{i}")
                if i != 15:
                    vop(lambda e, c=il: e.tensor_scalar(
                        frcols[:, c:c + 1], expnegm[:], rz[:, 0:1], None, ALU.mult),
                        reads=[zb], writes=[buf("frcols")], note=f"frc{h}_{i}")
                for g in range((i + 4) // 4):
                    pr, prb = (PTA, buf("PTA")) if g % 2 == 0 else (PTB, buf("PTB"))
                    j0 = g * 4
                    jn = min(4, i + 1 - j0)
                    for k in range(jn):
                        j = j0 + k
                        transpose_mm(pr[:, k * 128:(k + 1) * 128],
                                     p_sb[:, j * 128:(j + 1) * 128],
                                     [buf("p_sb")], [prb], f"pT{h}_{i}_{j}")
                    dst = pt[:, j0:j0 + jn, il * 128:(il + 1) * 128]
                    src = pr[:, 0:jn * 128].rearrange("p (a b) -> p a b", a=jn)
                    vop(lambda e, d=dst, s=src: e.tensor_copy(d, s),
                        reads=[prb], writes=[buf("pt"), prb], small=True,
                        note=f"ptev{h}_{i}_{g}")
            qb = buf("frr")
            nfr = 4 if qq < 3 else 3
            for k in range(nfr):
                B.op("pe", lambda e, k_=k: e.matmul(
                    PC[0:1, k_ * 128:(k_ + 1) * 128], frcols[:, k_:k_ + 1],
                    ident[:], start=True, stop=True),
                    reads=[buf("frcols"), buf("ident")], writes=[pcb],
                    small=False, note=f"frT{h}_{qq}_{k}")
            vop(lambda e, n=nfr: e.tensor_copy(frr[0:1, 0:n * 128],
                                               PC[0:1, 0:n * 128]),
                reads=[pcb], writes=[qb, pcb], small=True, note=f"frev{h}_{qq}")
            avb = buf("PAV")
            av_ops = []
            for j in range(qq * 4 + 4):
                c0 = (max(j, qq * 4) - qq * 4) * 128
                av_ops.append(("j", j, c0))
            for il in range(4):
                if qq * 4 + il != 15:
                    av_ops.append(("r", il, None))
            for idx, (kind, a, c0) in enumerate(av_ops):
                st = idx == 0
                sp = idx == len(av_ops) - 1
                if kind == "j":
                    mm(PAV[:, c0:512], vtm[:, a, :], pt[:, a, c0:512],
                       st, sp, [buf("vtm"), buf("pt")], [avb],
                       note=f"av{h}_{qq}_{a}")
                else:
                    i = qq * 4 + a
                    mm(PAV[:, a * 128:(a + 1) * 128],
                       sfr[0:1, i * 128:(i + 1) * 128], frr[0:1, a * 128:(a + 1) * 128],
                       st, sp, [buf("sfr"), qb], [avb], note=f"rk1{h}_{qq}_{a}")
            dst = att[:, h, qq * 512:(qq + 1) * 512]
            vop(lambda e, d=dst: e.tensor_copy(d, PAV[:]),
                reads=[avb], writes=[buf("attB"), avb], small=True,
                note=f"avev{h}_{qq}")
    if "att" in taps:
        dd = tap_dram("att", [128, H * T], BF16)
        sdma(dd[:], att.rearrange("p a b -> p (a b)"),
             reads=[buf("attB")], writes=[], note="tap_att")

    B.merge(buf("wo0"), [buf("wq")])
    B.merge(buf("wo1"), [buf("wk")])
    B.merge(buf("xinC0"), [buf("wv")])
    B.merge(buf("ofm"), [buf("p_sb")])
    B.merge(buf("otm"), [buf("y1fm")])

    def wo_dma(ech, bi):
        pdma(wo_b[bi].rearrange("p a b -> p (a b)"), attw_d[3, ech],
             reads=[], writes=[buf(f"wo{bi}")], note=f"wo{ech}")

    wo_dma(0, 0)
    wo_dma(1, 1)
    for ech in range(ND):
        bi = ech % 2
        wb = buf(f"wo{bi}")
        pb_a, pb_b = buf("PAA"), buf("PAB")
        for hch in range(H):
            for nch in range(4):
                ps = (PAA if nch < 2 else PAB)[:, (nch % 2) * 512:(nch % 2 + 1) * 512]
                mm(ps, wo_b[bi][:, hch, :], att[:, hch, nch * 512:(nch + 1) * 512],
                   hch == 0, hch == H - 1,
                   reads=[wb, buf("attB")], writes=[pb_a if nch < 2 else pb_b],
                   note=f"wo{ech}_{hch}_{nch}")
        if ech + 2 < ND:
            wo_dma(ech + 2, bi)
        ob = buf("ofm")
        vop(lambda e: e.tensor_copy(ofm_t[:, 0:1024], PAA[:]),
            reads=[pb_a], writes=[ob, pb_a], small=False, note=f"woevA{ech}")
        vop(lambda e: e.tensor_copy(ofm_t[:, 1024:2048], PAB[:]),
            reads=[pb_b], writes=[ob, pb_b], small=False, note=f"woevB{ech}")
        tile_to_fm(ofm_t, ob, otm, buf("otm"), (ech * 128, (ech + 1) * 128),
                   f"o_{ech}")
    B.merge(buf("x1B"), [buf("attB")])
    for t in range(NT):
        xb = buf(f"xinC{t % 2}")
        sdma(xinC[t % 2][:], x_d[t * 128:(t + 1) * 128, :],
             reads=[], writes=[xb], note=f"xreC{t}")
        par = t % 2
        sb = ln_stats([(otm[:, t, :], D)], [buf("otm")], par, D, f"ln2t{t}")
        fb = buf(f"ftile{par}")
        vop(lambda e, p=par, t_=t: e.tensor_scalar(
            ftile_bf[p][:], otm[:, t_, :], nmean[:, p:p + 1], istd[:, p:p + 1],
            ALU.add, ALU.mult),
            reads=[sb, buf("otm")], writes=[fb], small=False,
            note=f"ln2n{t}")
        vop(lambda e, p=par, t_=t: e.tensor_add(x1[:, t_, :], ftile_bf[p][:],
                                                xinC[p][:]),
            reads=[fb, xb], writes=[buf("x1B"), xb], small=False,
            note=f"res1_{t}")
    if "x1" in taps:
        dd = tap_dram("x1", [128, NT * D], BF16)
        sdma(dd[:], x1.rearrange("p a b -> p (a b)"),
             reads=[buf("x1B")], writes=[], note="tap_x1")

    B.merge(buf("y2fm"), [buf("q_sb"), buf("k_sb"), buf("vfm"), buf("vtm"),
                          buf("ofm"), buf("pt")])
    B.merge(buf("sdbcB"), [buf("pt")])
    B.merge(buf("guw0"), [buf("wo0"), buf("wo1")])
    B.merge(buf("guw1"), [buf("xinC0")])
    B.merge(buf("wd0"), [buf("xinC1")])
    B.merge(buf("wd1"), [buf("xinC1")])
    B.merge(buf("guB"), [buf("otm")])
    sdma(sdbc[:], sdbc_d, reads=[], writes=[buf("sdbcB")], note="ld_sdbc")

    def gu_dma(hch, bi):
        pdma(guw_b[bi].rearrange("p a b -> p (a b)"), gu8_d[hch],
             reads=[], writes=[buf(f"guw{bi}")], note=f"guw{hch}")

    def wd_dma(hch, bi):
        pdma(wd_b[bi], dn8_d[hch],
             reads=[], writes=[buf(f"wd{bi}")], note=f"wd{hch}")

    for gi, (t0, ntl) in enumerate(GROUPS):
        Wg = ntl * 128
        for tl in range(ntl):
            t = t0 + tl
            par = t % 2
            sb = ln_stats([(x1[:, t, :], D)], [buf("x1B")], par, D, f"ln3t{t}")
            fb = buf(f"ftile{par}")
            vop(lambda e, p=par, t_=t: e.tensor_scalar(
                ftile_bf[p][:], x1[:, t_, :], nmean[:, p:p + 1], istd[:, p:p + 1],
                ALU.add, ALU.mult),
                reads=[sb, buf("x1B")], writes=[fb], small=False, note=f"ln3n{t}")
            tile_to_fm(ftile_bf[par], fb, y2fm, buf("y2fm"),
                       (tl * 128, (tl + 1) * 128), f"y2_{t}")
        if gi == 0 and "ffn0" in taps:
            dd = tap_dram("y2g0", [128, ND * 768], BF16)
            sdma(dd[:], y2fm.rearrange("p a b -> p (a b)"),
                 reads=[buf("y2fm")], writes=[], note="tap_y2g0")
        gu_dma(0, 0)
        gu_dma(1, 1)
        for hch in range(NH):
            bi = hch % 2
            wbuf = buf(f"guw{bi}")
            vop(lambda e, b=bi: e.tensor_scalar(
                guw_b[b].rearrange("p a b -> p (a b)"),
                guw_b[b].rearrange("p a b -> p (a b)"),
                -128.0, None, ALU.add),
                reads=[wbuf], writes=[wbuf], small=False, note=f"deq{gi}_{hch}")
            pb_a, pb_b = buf("PAA"), buf("PAB")
            nchunks = [(a, min(512, Wg - a)) for a in range(0, Wg, 512)]
            for dch in range(ND):
                for (a, w_) in nchunks:
                    mm(PAA[:, a:a + w_], guw_b[bi][:, dch, :],
                       y2fm[:, dch, a:a + w_], dch == 0, dch == ND - 1,
                       [wbuf, buf("y2fm")], [pb_a], note=f"gate{gi}_{hch}_{a}")
            for dch in range(ND):
                for (a, w_) in nchunks:
                    mm(PAB[:, a:a + w_], guw_b[bi][:, 16 + dch, :],
                       y2fm[:, dch, a:a + w_], dch == 0, dch == ND - 1,
                       [wbuf, buf("y2fm")], [pb_b], note=f"up{gi}_{hch}_{a}")
            if hch + 2 < NH:
                gu_dma(hch + 2, bi)
            aop(lambda e, c=hch, w=Wg: e.activation(
                sqscr[:, 0:w], PAA[:, 0:w], AF.Silu, scale=sg_s[:, c:c + 1]),
                reads=[pb_a, buf("sg")], writes=[pb_a, buf("sqscr")],
                small=False, note=f"silu{gi}_{hch}")
            vop(lambda e, c=hch, w=Wg: e.scalar_tensor_tensor(
                gu[:, c, 0:w], PAB[:, 0:w], su_s[:, c:c + 1], sqscr[:, 0:w],
                ALU.mult, ALU.mult),
                reads=[pb_b, buf("su"), buf("sqscr")],
                writes=[buf("guB"), pb_b], small=False, note=f"guev{gi}_{hch}")
        if gi == 0 and "ffn0" in taps:
            dd = tap_dram("gug0", [128, NH * 768], BF16)
            sdma(dd[:], gu.rearrange("p a b -> p (a b)"),
                 reads=[buf("guB")], writes=[], note="tap_gug0")
        pairs = [(t0 + 2 * p, min(2, ntl - 2 * p)) for p in range((ntl + 1) // 2)]
        for (pt0, pn) in pairs:
            pcb2 = buf("PC")
            gb = buf("gusum")
            for k in range(pn):
                tl = pt0 - t0 + k
                for hch in range(NH):
                    B.op("pe", lambda e, k_=k, c=hch, a=tl * 128: e.matmul(
                        PC[:, k_ * 128:(k_ + 1) * 128], gu[:, c, a:a + 128],
                        ones1[:], start=(c == 0), stop=(c == NH - 1)),
                        reads=[buf("guB"), buf("ones1")], writes=[pcb2],
                        small=False, note=f"gus{pt0}_{k}_{hch}")
            for k in range(pn):
                vop(lambda e, k_=k: e.tensor_copy(gusum[:, k_:k_ + 1],
                                                  PC[:, k_ * 128:k_ * 128 + 1]),
                    reads=[pcb2], writes=[gb] if k == pn - 1 else [gb],
                    small=True, note=f"gusev{pt0}_{k}")
            vop(lambda e, n=pn: e.tensor_scalar(
                gsc[:, 0:n], gusum[:, 0:n], -128.0, None, ALU.mult),
                reads=[gb], writes=[gb], note=f"gsc{pt0}")
            wd_dma(0, 0)
            wd_dma(1, 1)
            t1_parts = [(PAV, "PAV", 0), (PTA, "PTA", 512),
                        (PTB, "PTB", 1024), (PC, "PC", 1536)]
            for hch in range(NH):
                bi = hch % 2
                wdb = buf(f"wd{bi}")
                st = hch == 0
                sp = hch == NH - 1
                tl = pt0 - t0
                for nch in range(4):
                    ps = (PAA if nch < 2 else PAB)[:, (nch % 2) * 512:(nch % 2 + 1) * 512]
                    mm(ps, gu[:, hch, tl * 128:(tl + 1) * 128],
                       wd_b[bi][:, nch * 512:(nch + 1) * 512], st, sp,
                       [buf("guB"), wdb],
                       [buf("PAA") if nch < 2 else buf("PAB")],
                       note=f"dn{pt0}_0_{hch}_{nch}")
                if pn > 1:
                    for (pp, pname, e0) in t1_parts:
                        mm(pp[:, 0:512], gu[:, hch, (tl + 1) * 128:(tl + 2) * 128],
                           wd_b[bi][:, e0:e0 + 512], st, sp,
                           [buf("guB"), wdb], [buf(pname)],
                           note=f"dn{pt0}_1_{hch}_{e0}")
                if hch + 2 < NH:
                    wd_dma(hch + 2, bi)
            for k in range(pn):
                t = pt0 + k
                par = t % 2
                if t == 2 and "ffn0" in taps:
                    dd = tap_dram("rawps2", [128, D], F32)
                    vop(lambda e: e.tensor_copy(fhalf_f32[0][:], PAA[:]),
                        reads=[buf("PAA")], writes=[buf("ftile0")], small=False,
                        note="tapc_a")
                    vop(lambda e: e.tensor_copy(fhalf_f32[1][:], PAB[:]),
                        reads=[buf("PAB")], writes=[buf("ftile1")], small=False,
                        note="tapc_b")
                    sdma(dd[:, 0:1024], fhalf_f32[0][:], reads=[buf("ftile0")],
                         writes=[], note="tap_rawa")
                    sdma(dd[:, 1024:2048], fhalf_f32[1][:], reads=[buf("ftile1")],
                         writes=[], note="tap_rawb")
                    dd3 = tap_dram("gsc2", [128, 2], F32)
                    sdma(dd3[:], gsc[:], reads=[buf("gusum")], writes=[],
                         note="tap_gsc2")
                if k == 0:
                    pieces_ps = [(PAA[:], 1024, buf("PAA")),
                                 (PAB[:], 1024, buf("PAB"))]
                else:
                    pieces_ps = [(PAV[:], 512, buf("PAV")), (PTA[:], 512, buf("PTA")),
                                 (PTB[:], 512, buf("PTB")), (PC[:], 512, buf("PC"))]
                off = 0
                for (pp, w_, pb) in pieces_ps:
                    half = 0 if off < 1024 else 1
                    dst = fhalf_f32[half][:, off % 1024:off % 1024 + w_]
                    sd = sdbc[:, off:off + w_]
                    fb = buf(f"ftile{half}")
                    vop(lambda e, d=dst, p=pp, s=sd, k_=k: e.scalar_tensor_tensor(
                        d, p, gsc[:, k_:k_ + 1], s, ALU.add, ALU.mult),
                        reads=[pb, buf("gusum"), buf("sdbcB")],
                        writes=[fb, pb], small=True, note=f"dsc{t}_{off}")
                    off += w_
                if t in (0, 2) and "ffn0" in taps:
                    dd = tap_dram(f"dn{t}", [128, D], F32)
                    sdma(dd[:, 0:1024], fhalf_f32[0][:], reads=[buf("ftile0")],
                         writes=[], note=f"tap_dn{t}a")
                    sdma(dd[:, 1024:2048], fhalf_f32[1][:], reads=[buf("ftile1")],
                         writes=[], note=f"tap_dn{t}b")
                    dd2 = tap_dram(f"gus{t}", [128, 2], F32)
                    sdma(dd2[:], gusum[:], reads=[buf("gusum")], writes=[],
                         note=f"tap_gus{t}")
                sb = ln_stats([(fhalf_f32[0][:], 1024), (fhalf_f32[1][:], 1024)],
                              [buf("ftile0"), buf("ftile1")], par, D, f"ln4t{t}")
                for half in range(2):
                    fb = buf(f"ftile{half}")
                    vop(lambda e, hf=half, p=par: e.tensor_scalar(
                        fhalf_f32[hf][:], fhalf_f32[hf][:],
                        nmean[:, p:p + 1], istd[:, p:p + 1], ALU.add, ALU.mult),
                        reads=[sb], writes=[fb], small=False,
                        note=f"ln4n{t}_{half}")
                    vop(lambda e, hf=half, t_=t: e.tensor_add(
                        fhalf_f32[hf][:], fhalf_f32[hf][:],
                        x1[:, t_, hf * 1024:(hf + 1) * 1024]),
                        reads=[buf("x1B")], writes=[fb], small=False,
                        note=f"res2_{t}_{half}")
                    sdma(y_d[t * 128:(t + 1) * 128, half * 1024:(half + 1) * 1024],
                         fhalf_f32[half][:], reads=[fb], writes=[fb],
                         note=f"out{t}_{half}")

    B.finalize()
    n_sp = sum(1 for o in B.ops if o.is_dma and o.eng == "sp")
    n_pl = sum(1 for o in B.ops if o.is_dma and o.eng == "pl")

    sems = {"pe": nc.alloc_semaphore("s_pe"), "dve": nc.alloc_semaphore("s_dve"),
            "act": nc.alloc_semaphore("s_act")}
    dma_sems = {"sp": nc.alloc_semaphore("d_sp"), "pl": nc.alloc_semaphore("d_pl")}

    with nc.Block() as block:
        @block.tensor
        def _(eng):
            B.emit_engine("pe", eng, sems, dma_sems)

        @block.vector
        def _(eng):
            B.emit_engine("dve", eng, sems, dma_sems)

        @block.scalar
        def _(eng):
            B.emit_engine("act", eng, sems, dma_sems)

        @block.sync
        def _(eng):
            B.emit_engine("sp", eng, sems, dma_sems)
            eng.wait_ge(dma_sems["sp"], 16 * n_sp)

        @block.gpsimd
        def _(eng):
            B.emit_engine("pl", eng, sems, dma_sems)
            eng.wait_ge(dma_sems["pl"], 16 * n_pl)

    return nc


def kernel(L0_attn_x, L0_attn_Wq, L0_attn_Wk, L0_attn_Wv, L0_attn_Wo,
           L0_ffn_wq_gate, L0_ffn_s_gate, L0_ffn_wq_up, L0_ffn_s_up,
           L0_ffn_wq_down, L0_ffn_s_down):
    x = np.ascontiguousarray(np.asarray(L0_attn_x, dtype=np.float32))
    packed = pack_inputs(L0_attn_Wq, L0_attn_Wk, L0_attn_Wv, L0_attn_Wo,
                         L0_ffn_wq_gate, L0_ffn_s_gate, L0_ffn_wq_up,
                         L0_ffn_s_up, L0_ffn_wq_down, L0_ffn_s_down)
    nc = build_program(DEBUG_TAPS)
    in_maps = []
    for c in range(8):
        m = {"x": np.ascontiguousarray(x[c])}
        m.update(packed)
        in_maps.append(m)
    res = run_bass_kernel_spmd(nc, in_maps, core_ids=list(range(8)))
    out = np.stack([np.asarray(res.results[c]["y"]) for c in range(8)], axis=0)
    return out


# version 02
# speedup vs baseline: 1.1091x; 1.1091x over previous
import numpy as np
import concourse.bass as bass
import concourse.mybir as mybir
from concourse.bass_utils import run_bass_kernel_spmd

F32 = mybir.dt.float32
BF16 = mybir.dt.bfloat16
U8 = mybir.dt.uint8
AF = mybir.ActivationFunctionType
ALU = mybir.AluOpType
AX = mybir.AxisListType

T = 2048
D = 2048
H = 16
HID = 5632
NT = T // 128
ND = D // 128
NH = HID // 128
ISQ = 1.0 / float(np.sqrt(2048.0))
EPS = 1e-5
GROUPS = [(0, 6), (6, 6), (12, 4)]

DEBUG_TAPS = ()



class Buf:
    __slots__ = ("name", "writer", "readers")

    def __init__(self, name):
        self.name = name
        self.writer = None
        self.readers = []


class Op:
    __slots__ = ("eng", "emit", "deps", "needs_inc", "incval", "seq",
                 "small", "is_dma", "dma_idx", "note")

    def __init__(self, eng, emit, small, is_dma, note):
        self.eng = eng
        self.emit = emit
        self.deps = []
        self.needs_inc = False
        self.incval = None
        self.seq = -1
        self.small = small
        self.is_dma = is_dma
        self.dma_idx = None
        self.note = note


class Builder:

    def __init__(self):
        self.ops = []

    def op(self, eng, emit, reads=(), writes=(), small=True, note=""):
        o = Op(eng, emit, small, False, note)
        for b in reads:
            if b.writer is not None:
                o.deps.append(b.writer)
            b.readers.append(o)
        for b in writes:
            o.deps.extend(b.readers)
            if b.writer is not None:
                o.deps.append(b.writer)
            b.readers = []
            b.writer = o
        o.seq = len(self.ops)
        self.ops.append(o)
        return o

    def dma(self, eng, emit, reads=(), writes=(), note=""):
        o = self.op(eng, emit, reads, writes, small=False, note=note)
        o.is_dma = True
        return o

    @staticmethod
    def merge(new_buf, old_bufs):
        for ob in old_bufs:
            if ob.writer is not None and ob.writer not in new_buf.readers:
                new_buf.readers.append(ob.writer)
            for r in ob.readers:
                if r not in new_buf.readers:
                    new_buf.readers.append(r)

    def finalize(self):
        for o in self.ops:
            o.deps = [d for d in o.deps if d is not o]
            for d in o.deps:
                if d.eng != o.eng and not d.is_dma:
                    d.needs_inc = True
        cnt = {}
        dma_cnt = {}
        for o in self.ops:
            if o.is_dma:
                dma_cnt[o.eng] = dma_cnt.get(o.eng, 0) + 1
                o.dma_idx = dma_cnt[o.eng]
            elif o.needs_inc:
                cnt[o.eng] = cnt.get(o.eng, 0) + 1
                o.incval = cnt[o.eng]

    def emit_engine(self, eng_name, eng, sems, dma_sems):
        waited = {}
        last_dp = None
        last_dp2 = None
        for o in self.ops:
            if o.eng != eng_name:
                continue
            need_nop = 0
            for d in o.deps:
                if d.eng == eng_name:
                    if d.small and not d.is_dma:
                        if d is last_dp:
                            need_nop = max(need_nop, 2)
                        elif d is last_dp2:
                            need_nop = max(need_nop, 1)
                    continue
                if d.is_dma:
                    sem = dma_sems[d.eng]
                    val = 16 * d.dma_idx
                else:
                    if d.incval is None:
                        raise RuntimeError(f"dep without inc: {d.note} <- {o.note}")
                    sem = sems[d.eng]
                    val = d.incval
                key = id(sem)
                if waited.get(key, 0) < val:
                    eng.wait_ge(sem, val)
                    waited[key] = val
            for _ in range(need_nop):
                if hasattr(eng, "engine_nop"):
                    eng.engine_nop()
                else:
                    eng.drain()
            if need_nop:
                last_dp = None
                last_dp2 = None
            ins = o.emit(eng)
            if o.is_dma:
                ins.then_inc(dma_sems[o.eng], 16)
            elif o.needs_inc:
                ins.then_inc(sems[o.eng], 1)
            last_dp2 = last_dp
            last_dp = o



def _u8(q):
    return np.ascontiguousarray(np.asarray(q).view(np.uint8)
                                .reshape(np.asarray(q).shape + (4,))[..., 0])


def pack_inputs(Wq, Wk, Wv, Wo, wq_gate, s_gate, wq_up, s_up, wq_down, s_down):
    import ml_dtypes
    out = {}
    aw = np.stack([np.asarray(w, dtype=np.float32) for w in (Wq, Wk, Wv, Wo)])
    aw = aw.reshape(4, ND, 128, ND, 128).transpose(0, 3, 2, 1, 4)
    out["attw"] = np.ascontiguousarray(aw.reshape(4, ND, 128, D))

    def pack_gu(q):
        w8 = _u8(q)
        w8 = w8.reshape(NH, 128, ND, 128).transpose(0, 3, 2, 1)
        return w8.reshape(NH, 128, D)

    g = pack_gu(wq_gate)
    u = pack_gu(wq_up)
    out["gu8"] = np.ascontiguousarray(np.concatenate([g, u], axis=2))
    d8 = _u8(wq_down)
    out["dn8"] = np.ascontiguousarray(d8.reshape(D, NH, 128).transpose(1, 2, 0))
    out["sg"] = np.ascontiguousarray(
        np.asarray(s_gate, dtype=np.float32).reshape(NH, 128).T)
    out["su"] = np.ascontiguousarray(
        np.asarray(s_up, dtype=np.float32).reshape(NH, 128).T)
    out["sdbc"] = np.ascontiguousarray(
        np.broadcast_to(np.asarray(s_down, dtype=np.float32)[None, :], (128, D)))
    out["ident"] = np.eye(128, dtype=ml_dtypes.bfloat16)
    m = np.zeros((128, 128), dtype=np.float32)
    a = np.arange(128)
    m[a[:, None] >= a[None, :]] = 1.0
    out["mask"] = m
    return out



def build_program(taps=()):
    nc = bass.Bass("TRN2", target_bir_lowering=False, debug=False)
    taps = set(taps)

    x_d = nc.dram_tensor("x", [T, D], F32, kind="ExternalInput").ap()
    attw_d = nc.dram_tensor("attw", [4, ND, 128, D], F32, kind="ExternalInput").ap()
    gu8_d = nc.dram_tensor("gu8", [NH, 128, 4096], U8, kind="ExternalInput").ap()
    dn8_d = nc.dram_tensor("dn8", [NH, 128, D], U8, kind="ExternalInput").ap()
    sg_d = nc.dram_tensor("sg", [128, NH], F32, kind="ExternalInput").ap()
    su_d = nc.dram_tensor("su", [128, NH], F32, kind="ExternalInput").ap()
    sdbc_d = nc.dram_tensor("sdbc", [128, D], F32, kind="ExternalInput").ap()
    ident_d = nc.dram_tensor("ident", [128, 128], BF16, kind="ExternalInput").ap()
    mask_d = nc.dram_tensor("mask", [128, 128], F32, kind="ExternalInput").ap()
    y_d = nc.dram_tensor("y", [T, D], F32, kind="ExternalOutput").ap()

    tap_d = {}

    def tap_dram(name, shape, dtype):
        if name not in tap_d:
            tap_d[name] = nc.dram_tensor("tap_" + name, shape, dtype,
                                         kind="ExternalOutput").ap()
        return tap_d[name]

    arenaA = nc.alloc_sbuf_tensor("arenaA", [128, 67584], U8)
    arenaB = nc.alloc_sbuf_tensor("arenaB", [128, 65536], U8)
    arenaD = nc.alloc_sbuf_tensor("arenaD", [128, 36864], U8)
    arenaE = nc.alloc_sbuf_tensor("arenaE", [128, 24576], U8)
    arenaF = nc.alloc_sbuf_tensor("arenaF", [128, 8192], U8)
    sqscr = nc.alloc_sbuf_tensor("sqscr", [128, 1024], BF16)
    ident = nc.alloc_sbuf_tensor("ident_s", [128, 128], BF16)
    mask = nc.alloc_sbuf_tensor("mask_s", [128, 128], F32)
    sg_s = nc.alloc_sbuf_tensor("sg_s", [128, NH], F32)
    su_s = nc.alloc_sbuf_tensor("su_s", [128, NH], F32)
    ones1 = nc.alloc_sbuf_tensor("ones1", [128, 128], BF16)

    def tiny(name, w=1, dt=F32):
        return nc.alloc_sbuf_tensor(name, [128, w], dt)

    m0 = tiny("m0", 2); mh2 = tiny("mh2", 2); bias = tiny("bias", 2)
    expnegm = tiny("expnegm", 2); zacc1 = tiny("zacc1", 2); zacc2 = tiny("zacc2", 2)
    zfin = tiny("zfin", 2); rz = tiny("rz", 2)
    frcols = tiny("frcols", 4, BF16)
    sfutc = tiny("sfutc", 16)
    sfuts = tiny("sfuts", 16, BF16)
    sfr = nc.alloc_sbuf_tensor("sfr", [1, T], BF16)
    frr = nc.alloc_sbuf_tensor("frr", [1, 512], BF16)
    ssum = tiny("ssum", 2); ssq = tiny("ssq", 2); varv = tiny("varv", 2)
    nmean = tiny("nmean", 2); istd = tiny("istd", 2)
    nt1 = tiny("nt1", 2); nt2 = tiny("nt2", 2)
    gusum = tiny("gusum", 2); gsc = tiny("gsc", 2)
    sufacc = tiny("sufacc", 1)

    def view(arena, byte_off, shape, dt):
        n = int(np.prod(shape[1:])) * mybir.dt.size(dt)
        ap = arena.ap()[:, byte_off:byte_off + n].bitcast(dt)
        if len(shape) == 2:
            return ap
        return ap.rearrange("p (a b) -> p a b", a=shape[1])

    y1fm = view(arenaA, 0, [128, ND, T], BF16)
    otm = view(arenaA, 0, [128, NT, D], BF16)
    gu = view(arenaA, 0, [128, NH, 768], BF16)
    att = view(arenaB, 0, [128, H, T], BF16)
    x1 = view(arenaB, 0, [128, NT, D], BF16)
    xin = [view(arenaD, 0, [128, D], F32), view(arenaD, 8192, [128, D], F32)]
    q_sb = view(arenaD, 0, [128, T], BF16)
    k_sb = view(arenaD, 4096, [128, T], BF16)
    vfm = view(arenaD, 8192, [128, T], BF16)
    vtm = view(arenaD, 12288, [128, ND, 128], BF16)
    p_sb2 = [view(arenaD, 16384, [128, T], BF16), view(arenaD, 8192, [128, T], BF16)]
    pt = view(arenaD, 20480, [128, 16, 512], BF16)
    ofm_t = view(arenaD, 16384, [128, T], BF16)
    y2fm = view(arenaD, 0, [128, ND, 768], BF16)
    sdbc = view(arenaD, 24576, [128, D], F32)
    wq_b = view(arenaE, 0, [128, ND, 128], BF16)
    wk_b = view(arenaE, 4096, [128, ND, 128], BF16)
    wv_b = view(arenaE, 8192, [128, ND, 128], BF16)
    wo_b = [view(arenaE, 0, [128, ND, 128], BF16),
            view(arenaE, 4096, [128, ND, 128], BF16)]
    xinC = [view(arenaE, 8192, [128, D], F32), view(arenaE, 16384, [128, D], F32)]
    guw_b = [view(arenaE, 0, [128, 32, 128], BF16),
             view(arenaE, 8192, [128, 32, 128], BF16)]
    wd_b = [view(arenaE, 16384, [128, D], BF16), view(arenaE, 20480, [128, D], BF16)]
    ftile_bf = [view(arenaF, 0, [128, D], BF16), view(arenaF, 4096, [128, D], BF16)]
    fhalf_f32 = [view(arenaF, 0, [128, 1024], F32), view(arenaF, 4096, [128, 1024], F32)]

    PAA = nc.alloc_psum_tensor("PAA", [128, 1024], F32)
    PAB = nc.alloc_psum_tensor("PAB", [128, 1024], F32)
    PAV = nc.alloc_psum_tensor("PAV", [128, 512], F32)
    PTA = nc.alloc_psum_tensor("PTA", [128, 512], F32)
    PTB = nc.alloc_psum_tensor("PTB", [128, 512], F32)
    PC = nc.alloc_psum_tensor("PC", [128, 512], F32)

    B = Builder()
    bufs = {}

    def buf(name):
        if name not in bufs:
            bufs[name] = Buf(name)
        return bufs[name]

    def mm(out_ap, lhsT_ap, rhs_ap, start, stop, reads, writes, note=""):
        B.op("pe", lambda e, o=out_ap, l=lhsT_ap, r=rhs_ap, st=start, sp=stop:
             e.matmul(o, l, r, start=st, stop=sp),
             reads=reads, writes=writes, small=False, note=note)

    def vop(fn, reads, writes, small=True, note=""):
        B.op("dve", fn, reads=reads, writes=writes, small=small, note=note)

    def aop(fn, reads, writes, small=True, note=""):
        B.op("act", fn, reads=reads, writes=writes, small=small, note=note)

    def sdma(dst, src, reads, writes, note=""):
        B.dma("sp", lambda e, d=dst, s=src: e.dma_start(d, s),
              reads=reads, writes=writes, note=note)

    def pdma(dst, src, reads, writes, note=""):
        B.dma("pl", lambda e, d=dst, s=src: e.dma_start(d, s),
              reads=reads, writes=writes, note=note)

    def ln_stats(pieces, srcbufs, par, n_feat, note):
        sb = buf("lnstat" + str(par))
        first = True
        for ap_, w_ in pieces:
            if first:
                vop(lambda e, a=ap_: e.reduce_sum(ssum[:, par:par + 1], a, axis=AX.X),
                    reads=list(srcbufs), writes=[sb], small=True, note=note + "sum")
                first = False
            else:
                vop(lambda e, a=ap_: e.reduce_sum(nt1[:, par:par + 1], a, axis=AX.X),
                    reads=list(srcbufs), writes=[sb], small=True, note=note + "sum2")
                vop(lambda e: e.tensor_add(ssum[:, par:par + 1],
                                           ssum[:, par:par + 1], nt1[:, par:par + 1]),
                    reads=[sb], writes=[sb], note=note + "sumadd")
        subpieces = []
        for ap_, w_ in pieces:
            for a0 in range(0, w_, 1024):
                subpieces.append((ap_[:, a0:min(a0 + 1024, w_)], min(1024, w_ - a0)))
        firstq = True
        for ap_, w_ in subpieces:
            za = ssq if firstq else nt2
            aop(lambda e, a=ap_, w=w_, z=za: e.activation(
                sqscr[:, 0:w], a, AF.Square, accum_out=z[:, par:par + 1]),
                reads=list(srcbufs), writes=[sb, buf("sqscr")], small=False,
                note=note + "sq")
            B.op("act", lambda e: e.drain(), reads=[sb], writes=[sb],
                 note=note + "sqdr")
            if not firstq:
                vop(lambda e: e.tensor_add(ssq[:, par:par + 1],
                                           ssq[:, par:par + 1], nt2[:, par:par + 1]),
                    reads=[sb], writes=[sb], note=note + "sqadd")
            firstq = False
        inv_n = 1.0 / float(n_feat)
        vop(lambda e: e.tensor_scalar(nt1[:, par:par + 1], ssum[:, par:par + 1],
                                      inv_n * inv_n, ssum[:, par:par + 1],
                                      ALU.mult, ALU.mult),
            reads=[sb], writes=[sb], note=note + "msq")
        vop(lambda e: e.tensor_scalar(varv[:, par:par + 1], ssq[:, par:par + 1],
                                      inv_n, None, ALU.mult),
            reads=[sb], writes=[sb], note=note + "var0")
        vop(lambda e: e.tensor_sub(varv[:, par:par + 1], varv[:, par:par + 1],
                                   nt1[:, par:par + 1]),
            reads=[sb], writes=[sb], note=note + "var1")
        vop(lambda e: e.tensor_scalar(varv[:, par:par + 1], varv[:, par:par + 1],
                                      1.0, EPS, ALU.mult, ALU.add),
            reads=[sb], writes=[sb], note=note + "var2")
        vop(lambda e: e.tensor_scalar(nmean[:, par:par + 1], ssum[:, par:par + 1],
                                      -inv_n, None, ALU.mult),
            reads=[sb], writes=[sb], note=note + "nmean")
        aop(lambda e: e.activation(istd[:, par:par + 1], varv[:, par:par + 1],
                                   AF.Sqrt),
            reads=[sb], writes=[sb], note=note + "sqrt")
        vop(lambda e: e.reciprocal(istd[:, par:par + 1], istd[:, par:par + 1]),
            reads=[sb], writes=[sb], note=note + "recip")
        vop(lambda e: e.tensor_scalar(nt1[:, par:par + 1], istd[:, par:par + 1],
                                      1.0, istd[:, par:par + 1], ALU.mult, ALU.mult),
            reads=[sb], writes=[sb], note=note + "n0")
        vop(lambda e: e.tensor_scalar(nt1[:, par:par + 1], nt1[:, par:par + 1],
                                      varv[:, par:par + 1], None, ALU.mult),
            reads=[sb], writes=[sb], note=note + "n1")
        vop(lambda e: e.tensor_scalar(nt1[:, par:par + 1], nt1[:, par:par + 1],
                                      -0.5, 1.5, ALU.mult, ALU.add),
            reads=[sb], writes=[sb], note=note + "n2")
        vop(lambda e: e.tensor_scalar(istd[:, par:par + 1], istd[:, par:par + 1],
                                      nt1[:, par:par + 1], None, ALU.mult),
            reads=[sb], writes=[sb], note=note + "n3")
        return sb

    def transpose_mm(psum_slice, data_slice, reads, writes, note):
        B.op("pe", lambda e, o=psum_slice, l=data_slice:
             e.matmul(o, l, ident[:], start=True, stop=True),
             reads=list(reads) + [buf("ident")], writes=writes, small=False,
             note=note)

    sdma(ident[:], ident_d, reads=[], writes=[buf("ident")], note="ld_ident")
    sdma(mask[:], mask_d, reads=[], writes=[buf("mask")], note="ld_mask")
    sdma(sg_s[:], sg_d, reads=[], writes=[buf("sg")], note="ld_sg")
    sdma(su_s[:], su_d, reads=[], writes=[buf("su")], note="ld_su")
    vop(lambda e: e.memset(ones1[:], 1.0), reads=[], writes=[buf("ones1")],
        note="ones1")

    def tile_to_fm(src_ap, src_buf, dst_view, dst_buf, tcols, tag):
        for g in range(4):
            pr, prb = (PTA, buf("PTA")) if g % 2 == 0 else (PTB, buf("PTB"))
            for k in range(4):
                dch = g * 4 + k
                transpose_mm(pr[:, k * 128:(k + 1) * 128],
                             src_ap[:, dch * 128:(dch + 1) * 128],
                             [src_buf], [prb], f"{tag}T{dch}")
            dst = dst_view[:, g * 4:(g + 1) * 4, tcols[0]:tcols[1]]
            src = pr[:].rearrange("p (a b) -> p a b", a=4)
            vop(lambda e, d=dst, s=src: e.tensor_copy(d, s),
                reads=[prb], writes=[dst_buf, prb], small=True,
                note=f"{tag}ev{g}")

    for t in range(NT):
        xb = buf(f"xin{t % 2}")
        sdma(xin[t % 2][:], x_d[t * 128:(t + 1) * 128, :],
             reads=[], writes=[xb], note=f"xin{t}")
        par = t % 2
        sb = ln_stats([(xin[par][:], D)], [xb], par, D, f"ln1t{t}")
        fb = buf(f"ftile{par}")
        vop(lambda e, p=par: e.tensor_scalar(
            ftile_bf[p][:], xin[p][:], nmean[:, p:p + 1], istd[:, p:p + 1],
            ALU.add, ALU.mult),
            reads=[sb, xb], writes=[fb], small=False, note=f"ln1n{t}")
        tile_to_fm(ftile_bf[par], fb, y1fm, buf("y1fm"),
                   (t * 128, (t + 1) * 128), f"y1_{t}")
        if t == 0 and "dbg0" in taps:
            dd = tap_dram("xin0", [128, D], F32)
            sdma(dd[:], xin[0][:], reads=[xb], writes=[], note="tap_xin0")
            dd = tap_dram("ftile0", [128, D], BF16)
            sdma(dd[:], ftile_bf[0][:], reads=[fb], writes=[], note="tap_ft0")
            dd = tap_dram("stat0", [128, 8], F32)
            for ci, vv in enumerate((ssum, ssq, varv, istd)):
                sdma(dd[:, ci * 2:(ci + 1) * 2], vv[:], reads=[sb], writes=[],
                     note=f"tap_st{ci}")
    if "y1fm" in taps:
        dd = tap_dram("y1fm", [128, ND * T], BF16)
        sdma(dd[:], y1fm.rearrange("p a b -> p (a b)"),
             reads=[buf("y1fm")], writes=[], note="tap_y1fm")

    B.merge(buf("q_sb"), [buf("xin0")])
    B.merge(buf("k_sb"), [buf("xin0")])
    B.merge(buf("vfm"), [buf("xin1")])
    B.merge(buf("vtm"), [buf("xin1")])

    def head_w_dma(h):
        for wi, (wb, nm) in enumerate(((wq_b, "wq"), (wk_b, "wk"), (wv_b, "wv"))):
            pdma(wb.rearrange("p a b -> p (a b)"), attw_d[wi, h],
                 reads=[], writes=[buf(nm)], note=f"w{nm}{h}")

    head_w_dma(0)

    for h in range(H):
        for (wbuf_ap, wname, outsb, outbuf) in (
                (wq_b, "wq", q_sb, "q_sb"), (wk_b, "wk", k_sb, "k_sb"),
                (wv_b, "wv", vfm, "vfm")):
            pb_a, pb_b = buf("PAA"), buf("PAB")
            for dch in range(ND):
                for nch in range(4):
                    ps = (PAA if nch < 2 else PAB)[:, (nch % 2) * 512:(nch % 2 + 1) * 512]
                    mm(ps, wbuf_ap[:, dch, :], y1fm[:, dch, nch * 512:(nch + 1) * 512],
                       dch == 0, dch == ND - 1,
                       reads=[buf(wname), buf("y1fm")],
                       writes=[pb_a if nch < 2 else pb_b],
                       note=f"{wname}mm{h}_{dch}_{nch}")
            ob = buf(outbuf)
            vop(lambda e, o=outsb: e.tensor_copy(o[:, 0:1024], PAA[:]),
                reads=[pb_a], writes=[ob, pb_a], small=False, note=f"{wname}evA{h}")
            vop(lambda e, o=outsb: e.tensor_copy(o[:, 1024:2048], PAB[:]),
                reads=[pb_b], writes=[ob, pb_b], small=False, note=f"{wname}evB{h}")
        if h + 1 < H:
            head_w_dma(h + 1)
        for g in range(4):
            pr, prb = (PTA, buf("PTA")) if g % 2 == 0 else (PTB, buf("PTB"))
            for k in range(4):
                j = g * 4 + k
                transpose_mm(pr[:, k * 128:(k + 1) * 128],
                             vfm[:, j * 128:(j + 1) * 128],
                             [buf("vfm")], [prb], f"vT{h}_{j}")
            dst = vtm[:, g * 4:(g + 1) * 4, :]
            vop(lambda e, d=dst, s=pr[:].rearrange("p (a b) -> p a b", a=4):
                e.tensor_copy(d, s),
                reads=[prb], writes=[buf("vtm"), prb], small=True,
                note=f"vev{h}_{g}")
        sfb = buf("sfut")
        for j in range(ND):
            vop(lambda e, j_=j: e.reduce_sum(
                sfutc[:, j_:j_ + 1], vfm[:, j_ * 128:(j_ + 1) * 128], axis=AX.X),
                reads=[buf("vfm")], writes=[sfb], note=f"cs{h}_{j}")
        vop(lambda e: e.memset(sfuts[:, 15:16], 0.0), reads=[], writes=[sfb],
            note=f"suf15_{h}")
        vop(lambda e: e.memset(sufacc[:], 0.0), reads=[], writes=[sfb],
            note=f"sufa_{h}")
        for i in range(14, -1, -1):
            vop(lambda e, i_=i: e.tensor_add(
                sufacc[:], sufacc[:], sfutc[:, i_ + 1:i_ + 2]),
                reads=[sfb], writes=[sfb], note=f"sufadd{h}_{i}")
            vop(lambda e, i_=i: e.tensor_copy(sfuts[:, i_:i_ + 1], sufacc[:]),
                reads=[sfb], writes=[sfb], note=f"sufcp{h}_{i}")
        pcb = buf("PC")
        for g4 in range(4):
            for k in range(4):
                i = g4 * 4 + k
                B.op("pe", lambda e, i_=i, k_=k: e.matmul(
                    PC[0:1, k_ * 128:(k_ + 1) * 128], sfuts[:, i_:i_ + 1],
                    ident[:], start=True, stop=True),
                    reads=[sfb, buf("ident")], writes=[pcb], small=False,
                    note=f"sfT{h}_{i}")
            vop(lambda e, g_=g4: e.tensor_copy(
                sfr[0:1, g_ * 512:(g_ + 1) * 512], PC[0:1, 0:512]),
                reads=[pcb], writes=[buf("sfr"), pcb], small=True,
                note=f"sfev{h}_{g4}")

        def emit_S(i):
            W = (i + 1) * 128
            if i < 8:
                halves = [((PAA, "PAA") if i % 2 == 0 else (PAB, "PAB")) + (0, W)]
            else:
                halves = [(PAA, "PAA", 0, 1024), (PAB, "PAB", 1024, W)]
            for (pp, pname, c0, c1) in halves:
                hb = buf(pname)
                for a in range(c0, c1, 512):
                    bw = min(512, c1 - a)
                    mm(pp[:, a - c0:a - c0 + bw],
                       q_sb[:, i * 128:(i + 1) * 128], k_sb[:, a:a + bw],
                       True, True, [buf("q_sb"), buf("k_sb")], [hb],
                       note=f"S{h}_{i}_{a}")
            return halves

        def emit_premax(i, halves):
            c = i % 2
            dpp, dname, dc0, _ = halves[-1]
            dhb = buf(dname)
            doff = i * 128 - dc0
            vop(lambda e, p=dpp, o=doff: e.tensor_mul(
                p[:, o:o + 128], p[:, o:o + 128], mask[:]),
                reads=[buf("mask")], writes=[dhb], small=True,
                note=f"mask{h}_{i}")
            smb = buf(f"smax{c}")
            first = True
            for (pp, pname, c0, c1) in halves:
                tgt = m0 if first else mh2
                vop(lambda e, p=pp, w=c1 - c0, tg=tgt: e.reduce_max(
                    tg[:, c:c + 1], p[:, 0:w], axis=AX.X),
                    reads=[buf(pname)], writes=[smb], small=True,
                    note=f"max{h}_{i}")
                first = False
            if len(halves) > 1:
                vop(lambda e: e.tensor_max(m0[:, c:c + 1], m0[:, c:c + 1],
                                           mh2[:, c:c + 1]),
                    reads=[smb], writes=[smb], note=f"maxc{h}_{i}")
            vop(lambda e: e.tensor_scalar_max(m0[:, c:c + 1], m0[:, c:c + 1], 0.0),
                reads=[smb], writes=[smb], note=f"relu{h}_{i}")
            vop(lambda e: e.tensor_scalar(bias[:, c:c + 1], m0[:, c:c + 1],
                                          -ISQ, None, ALU.mult),
                reads=[smb], writes=[smb], note=f"bias{h}_{i}")
            return smb

        def emit_exp(i, halves, smb):
            c = i % 2
            zb = buf(f"zacc{c}")
            psb = p_sb2[c]
            psbuf = buf("vfm" if c == 1 else "p_sb")
            first = True
            for (pp, pname, c0, c1) in halves:
                za = zacc1 if first else zacc2
                aop(lambda e, p=pp, a=c0, w=c1 - c0, z=za, o=psb: e.activation(
                    o[:, a:a + w], p[:, 0:w], AF.Exp,
                    bias=bias[:, c:c + 1], scale=ISQ, accum_out=z[:, c:c + 1]),
                    reads=[smb, buf(pname)], writes=[psbuf, zb, buf(pname)],
                    small=False, note=f"exp{h}_{i}")
                first = False
            aop(lambda e: e.activation(expnegm[:, c:c + 1], bias[:, c:c + 1],
                                       AF.Exp),
                reads=[smb], writes=[zb], note=f"enm{h}_{i}")
            B.op("act", lambda e: e.drain(), reads=[zb], writes=[zb],
                 note=f"zdr{h}_{i}")
            return zb, psb, psbuf

        def emit_znorm(i, halves, zb, psb, psbuf):
            c = i % 2
            W = (i + 1) * 128
            nf = float((15 - i) * 128)
            if len(halves) > 1:
                vop(lambda e: e.tensor_add(zacc1[:, c:c + 1], zacc1[:, c:c + 1],
                                           zacc2[:, c:c + 1]),
                    reads=[zb], writes=[zb], note=f"zc{h}_{i}")
            if nf > 0:
                vop(lambda e, n=nf: e.tensor_scalar(
                    zfin[:, c:c + 1], expnegm[:, c:c + 1], n, None, ALU.mult),
                    reads=[zb], writes=[zb], note=f"znf{h}_{i}")
                vop(lambda e: e.tensor_add(zacc1[:, c:c + 1], zacc1[:, c:c + 1],
                                           zfin[:, c:c + 1]),
                    reads=[zb], writes=[zb], note=f"zadd{h}_{i}")
            vop(lambda e: e.reciprocal(rz[:, c:c + 1], zacc1[:, c:c + 1]),
                reads=[zb], writes=[zb], note=f"rz{h}_{i}")
            vop(lambda e, w=W, o=psb: e.tensor_scalar(
                o[:, 0:w], o[:, 0:w], rz[:, c:c + 1], None, ALU.mult),
                reads=[zb], writes=[psbuf], small=False,
                note=f"pscl{h}_{i}")
            if i != 15:
                vop(lambda e, q_=i % 4: e.tensor_scalar(
                    frcols[:, q_:q_ + 1], expnegm[:, c:c + 1], rz[:, c:c + 1],
                    None, ALU.mult),
                    reads=[zb], writes=[buf("frcols")], note=f"frc{h}_{i}")

        def emit_T(i):
            il = i % 4
            psb = p_sb2[i % 2]
            psbuf = buf("vfm" if i % 2 == 1 else "p_sb")
            for g in range((i + 4) // 4):
                pr, prb = (PTA, buf("PTA")) if g % 2 == 0 else (PTB, buf("PTB"))
                j0 = g * 4
                jn = min(4, i + 1 - j0)
                for k in range(jn):
                    j = j0 + k
                    transpose_mm(pr[:, k * 128:(k + 1) * 128],
                                 psb[:, j * 128:(j + 1) * 128],
                                 [psbuf], [prb], f"pT{h}_{i}_{j}")
                dst = pt[:, j0:j0 + jn, il * 128:(il + 1) * 128]
                src2 = pr[:, 0:jn * 128].rearrange("p (a b) -> p a b", a=jn)
                vop(lambda e, d=dst, s=src2: e.tensor_copy(d, s),
                    reads=[prb], writes=[buf("pt"), prb], small=True,
                    note=f"ptev{h}_{i}_{g}")

        def emit_AV(qq):
            qb = buf("frr")
            nfr = 4 if qq < 3 else 3
            for k in range(nfr):
                B.op("pe", lambda e, k_=k: e.matmul(
                    PC[0:1, k_ * 128:(k_ + 1) * 128], frcols[:, k_:k_ + 1],
                    ident[:], start=True, stop=True),
                    reads=[buf("frcols"), buf("ident")], writes=[pcb],
                    small=False, note=f"frT{h}_{qq}_{k}")
            vop(lambda e, n=nfr: e.tensor_copy(frr[0:1, 0:n * 128],
                                               PC[0:1, 0:n * 128]),
                reads=[pcb], writes=[qb, pcb], small=True, note=f"frev{h}_{qq}")
            avb = buf("PAV")
            av_ops = []
            for j in range(qq * 4 + 4):
                c0 = (max(j, qq * 4) - qq * 4) * 128
                av_ops.append(("j", j, c0))
            for il in range(4):
                if qq * 4 + il != 15:
                    av_ops.append(("r", il, None))
            for idx, (kind, a, c0) in enumerate(av_ops):
                st = idx == 0
                sp = idx == len(av_ops) - 1
                if kind == "j":
                    mm(PAV[:, c0:512], vtm[:, a, :], pt[:, a, c0:512],
                       st, sp, [buf("vtm"), buf("pt")], [avb],
                       note=f"av{h}_{qq}_{a}")
                else:
                    i = qq * 4 + a
                    mm(PAV[:, a * 128:(a + 1) * 128],
                       sfr[0:1, i * 128:(i + 1) * 128],
                       frr[0:1, a * 128:(a + 1) * 128],
                       st, sp, [buf("sfr"), qb], [avb], note=f"rk1{h}_{qq}_{a}")
            dst = att[:, h, qq * 512:(qq + 1) * 512]
            vop(lambda e, d=dst: e.tensor_copy(d, PAV[:]),
                reads=[avb], writes=[buf("attB"), avb], small=True,
                note=f"avev{h}_{qq}")

        pending = {}
        for i in range(17):
            if i < 16:
                halves = emit_S(i)
                smb = emit_premax(i, halves)
                pending[i] = (halves,) + emit_exp(i, halves, smb)
            if i >= 1:
                emit_T(i - 1)
            if i >= 4 and i % 4 == 0:
                emit_AV(i // 4 - 1)
            if i < 16:
                emit_znorm(i, *pending.pop(i))
    if "att" in taps:
        dd = tap_dram("att", [128, H * T], BF16)
        sdma(dd[:], att.rearrange("p a b -> p (a b)"),
             reads=[buf("attB")], writes=[], note="tap_att")

    B.merge(buf("wo0"), [buf("wq")])
    B.merge(buf("wo1"), [buf("wk")])
    B.merge(buf("xinC0"), [buf("wv")])
    B.merge(buf("ofm"), [buf("p_sb")])
    B.merge(buf("otm"), [buf("y1fm")])

    def wo_dma(ech, bi):
        pdma(wo_b[bi].rearrange("p a b -> p (a b)"), attw_d[3, ech],
             reads=[], writes=[buf(f"wo{bi}")], note=f"wo{ech}")

    wo_dma(0, 0)
    wo_dma(1, 1)
    for ech in range(ND):
        bi = ech % 2
        wb = buf(f"wo{bi}")
        pb_a, pb_b = buf("PAA"), buf("PAB")
        for hch in range(H):
            for nch in range(4):
                ps = (PAA if nch < 2 else PAB)[:, (nch % 2) * 512:(nch % 2 + 1) * 512]
                mm(ps, wo_b[bi][:, hch, :], att[:, hch, nch * 512:(nch + 1) * 512],
                   hch == 0, hch == H - 1,
                   reads=[wb, buf("attB")], writes=[pb_a if nch < 2 else pb_b],
                   note=f"wo{ech}_{hch}_{nch}")
        if ech + 2 < ND:
            wo_dma(ech + 2, bi)
        ob = buf("ofm")
        vop(lambda e: e.tensor_copy(ofm_t[:, 0:1024], PAA[:]),
            reads=[pb_a], writes=[ob, pb_a], small=False, note=f"woevA{ech}")
        vop(lambda e: e.tensor_copy(ofm_t[:, 1024:2048], PAB[:]),
            reads=[pb_b], writes=[ob, pb_b], small=False, note=f"woevB{ech}")
        tile_to_fm(ofm_t, ob, otm, buf("otm"), (ech * 128, (ech + 1) * 128),
                   f"o_{ech}")
    B.merge(buf("x1B"), [buf("attB")])
    for t in range(NT):
        xb = buf(f"xinC{t % 2}")
        sdma(xinC[t % 2][:], x_d[t * 128:(t + 1) * 128, :],
             reads=[], writes=[xb], note=f"xreC{t}")
        par = t % 2
        sb = ln_stats([(otm[:, t, :], D)], [buf("otm")], par, D, f"ln2t{t}")
        fb = buf(f"ftile{par}")
        vop(lambda e, p=par, t_=t: e.tensor_scalar(
            ftile_bf[p][:], otm[:, t_, :], nmean[:, p:p + 1], istd[:, p:p + 1],
            ALU.add, ALU.mult),
            reads=[sb, buf("otm")], writes=[fb], small=False,
            note=f"ln2n{t}")
        vop(lambda e, p=par, t_=t: e.tensor_add(x1[:, t_, :], ftile_bf[p][:],
                                                xinC[p][:]),
            reads=[fb, xb], writes=[buf("x1B"), xb], small=False,
            note=f"res1_{t}")
    if "x1" in taps:
        dd = tap_dram("x1", [128, NT * D], BF16)
        sdma(dd[:], x1.rearrange("p a b -> p (a b)"),
             reads=[buf("x1B")], writes=[], note="tap_x1")

    B.merge(buf("y2fm"), [buf("q_sb"), buf("k_sb"), buf("vfm"), buf("vtm"),
                          buf("ofm"), buf("pt")])
    B.merge(buf("sdbcB"), [buf("pt")])
    B.merge(buf("guw0"), [buf("wo0"), buf("wo1")])
    B.merge(buf("guw1"), [buf("xinC0")])
    B.merge(buf("wd0"), [buf("xinC1")])
    B.merge(buf("wd1"), [buf("xinC1")])
    B.merge(buf("guB"), [buf("otm")])
    sdma(sdbc[:], sdbc_d, reads=[], writes=[buf("sdbcB")], note="ld_sdbc")

    def gu_dma(hch, bi):
        pdma(guw_b[bi].rearrange("p a b -> p (a b)"), gu8_d[hch],
             reads=[], writes=[buf(f"guw{bi}")], note=f"guw{hch}")

    def wd_dma(hch, bi):
        pdma(wd_b[bi], dn8_d[hch],
             reads=[], writes=[buf(f"wd{bi}")], note=f"wd{hch}")

    for gi, (t0, ntl) in enumerate(GROUPS):
        Wg = ntl * 128
        for tl in range(ntl):
            t = t0 + tl
            par = t % 2
            sb = ln_stats([(x1[:, t, :], D)], [buf("x1B")], par, D, f"ln3t{t}")
            fb = buf(f"ftile{par}")
            vop(lambda e, p=par, t_=t: e.tensor_scalar(
                ftile_bf[p][:], x1[:, t_, :], nmean[:, p:p + 1], istd[:, p:p + 1],
                ALU.add, ALU.mult),
                reads=[sb, buf("x1B")], writes=[fb], small=False, note=f"ln3n{t}")
            tile_to_fm(ftile_bf[par], fb, y2fm, buf("y2fm"),
                       (tl * 128, (tl + 1) * 128), f"y2_{t}")
        if gi == 0 and "ffn0" in taps:
            dd = tap_dram("y2g0", [128, ND * 768], BF16)
            sdma(dd[:], y2fm.rearrange("p a b -> p (a b)"),
                 reads=[buf("y2fm")], writes=[], note="tap_y2g0")
        gu_dma(0, 0)
        gu_dma(1, 1)
        for hch in range(NH):
            bi = hch % 2
            wbuf = buf(f"guw{bi}")
            vop(lambda e, b=bi: e.tensor_scalar(
                guw_b[b].rearrange("p a b -> p (a b)"),
                guw_b[b].rearrange("p a b -> p (a b)"),
                -128.0, None, ALU.add),
                reads=[wbuf], writes=[wbuf], small=False, note=f"deq{gi}_{hch}")
            pb_a, pb_b = buf("PAA"), buf("PAB")
            nchunks = [(a, min(512, Wg - a)) for a in range(0, Wg, 512)]
            for dch in range(ND):
                for (a, w_) in nchunks:
                    mm(PAA[:, a:a + w_], guw_b[bi][:, dch, :],
                       y2fm[:, dch, a:a + w_], dch == 0, dch == ND - 1,
                       [wbuf, buf("y2fm")], [pb_a], note=f"gate{gi}_{hch}_{a}")
            for dch in range(ND):
                for (a, w_) in nchunks:
                    mm(PAB[:, a:a + w_], guw_b[bi][:, 16 + dch, :],
                       y2fm[:, dch, a:a + w_], dch == 0, dch == ND - 1,
                       [wbuf, buf("y2fm")], [pb_b], note=f"up{gi}_{hch}_{a}")
            if hch + 2 < NH:
                gu_dma(hch + 2, bi)
            aop(lambda e, c=hch, w=Wg: e.activation(
                sqscr[:, 0:w], PAA[:, 0:w], AF.Silu, scale=sg_s[:, c:c + 1]),
                reads=[pb_a, buf("sg")], writes=[pb_a, buf("sqscr")],
                small=False, note=f"silu{gi}_{hch}")
            vop(lambda e, c=hch, w=Wg: e.scalar_tensor_tensor(
                gu[:, c, 0:w], PAB[:, 0:w], su_s[:, c:c + 1], sqscr[:, 0:w],
                ALU.mult, ALU.mult),
                reads=[pb_b, buf("su"), buf("sqscr")],
                writes=[buf("guB"), pb_b], small=False, note=f"guev{gi}_{hch}")
        if gi == 0 and "ffn0" in taps:
            dd = tap_dram("gug0", [128, NH * 768], BF16)
            sdma(dd[:], gu.rearrange("p a b -> p (a b)"),
                 reads=[buf("guB")], writes=[], note="tap_gug0")
        pairs = [(t0 + 2 * p, min(2, ntl - 2 * p)) for p in range((ntl + 1) // 2)]
        for (pt0, pn) in pairs:
            pcb2 = buf("PC")
            gb = buf("gusum")
            for k in range(pn):
                tl = pt0 - t0 + k
                for hch in range(NH):
                    B.op("pe", lambda e, k_=k, c=hch, a=tl * 128: e.matmul(
                        PC[:, k_ * 128:(k_ + 1) * 128], gu[:, c, a:a + 128],
                        ones1[:], start=(c == 0), stop=(c == NH - 1)),
                        reads=[buf("guB"), buf("ones1")], writes=[pcb2],
                        small=False, note=f"gus{pt0}_{k}_{hch}")
            for k in range(pn):
                vop(lambda e, k_=k: e.tensor_copy(gusum[:, k_:k_ + 1],
                                                  PC[:, k_ * 128:k_ * 128 + 1]),
                    reads=[pcb2], writes=[gb] if k == pn - 1 else [gb],
                    small=True, note=f"gusev{pt0}_{k}")
            vop(lambda e, n=pn: e.tensor_scalar(
                gsc[:, 0:n], gusum[:, 0:n], -128.0, None, ALU.mult),
                reads=[gb], writes=[gb], note=f"gsc{pt0}")
            wd_dma(0, 0)
            wd_dma(1, 1)
            t1_parts = [(PAV, "PAV", 0), (PTA, "PTA", 512),
                        (PTB, "PTB", 1024), (PC, "PC", 1536)]
            for hch in range(NH):
                bi = hch % 2
                wdb = buf(f"wd{bi}")
                st = hch == 0
                sp = hch == NH - 1
                tl = pt0 - t0
                for nch in range(4):
                    ps = (PAA if nch < 2 else PAB)[:, (nch % 2) * 512:(nch % 2 + 1) * 512]
                    mm(ps, gu[:, hch, tl * 128:(tl + 1) * 128],
                       wd_b[bi][:, nch * 512:(nch + 1) * 512], st, sp,
                       [buf("guB"), wdb],
                       [buf("PAA") if nch < 2 else buf("PAB")],
                       note=f"dn{pt0}_0_{hch}_{nch}")
                if pn > 1:
                    for (pp, pname, e0) in t1_parts:
                        mm(pp[:, 0:512], gu[:, hch, (tl + 1) * 128:(tl + 2) * 128],
                           wd_b[bi][:, e0:e0 + 512], st, sp,
                           [buf("guB"), wdb], [buf(pname)],
                           note=f"dn{pt0}_1_{hch}_{e0}")
                if hch + 2 < NH:
                    wd_dma(hch + 2, bi)
            for k in range(pn):
                t = pt0 + k
                par = t % 2
                if t == 2 and "ffn0" in taps:
                    dd = tap_dram("rawps2", [128, D], F32)
                    vop(lambda e: e.tensor_copy(fhalf_f32[0][:], PAA[:]),
                        reads=[buf("PAA")], writes=[buf("ftile0")], small=False,
                        note="tapc_a")
                    vop(lambda e: e.tensor_copy(fhalf_f32[1][:], PAB[:]),
                        reads=[buf("PAB")], writes=[buf("ftile1")], small=False,
                        note="tapc_b")
                    sdma(dd[:, 0:1024], fhalf_f32[0][:], reads=[buf("ftile0")],
                         writes=[], note="tap_rawa")
                    sdma(dd[:, 1024:2048], fhalf_f32[1][:], reads=[buf("ftile1")],
                         writes=[], note="tap_rawb")
                    dd3 = tap_dram("gsc2", [128, 2], F32)
                    sdma(dd3[:], gsc[:], reads=[buf("gusum")], writes=[],
                         note="tap_gsc2")
                if k == 0:
                    pieces_ps = [(PAA[:], 1024, buf("PAA")),
                                 (PAB[:], 1024, buf("PAB"))]
                else:
                    pieces_ps = [(PAV[:], 512, buf("PAV")), (PTA[:], 512, buf("PTA")),
                                 (PTB[:], 512, buf("PTB")), (PC[:], 512, buf("PC"))]
                off = 0
                for (pp, w_, pb) in pieces_ps:
                    half = 0 if off < 1024 else 1
                    dst = fhalf_f32[half][:, off % 1024:off % 1024 + w_]
                    sd = sdbc[:, off:off + w_]
                    fb = buf(f"ftile{half}")
                    vop(lambda e, d=dst, p=pp, s=sd, k_=k: e.scalar_tensor_tensor(
                        d, p, gsc[:, k_:k_ + 1], s, ALU.add, ALU.mult),
                        reads=[pb, buf("gusum"), buf("sdbcB")],
                        writes=[fb, pb], small=True, note=f"dsc{t}_{off}")
                    off += w_
                if t in (0, 2) and "ffn0" in taps:
                    dd = tap_dram(f"dn{t}", [128, D], F32)
                    sdma(dd[:, 0:1024], fhalf_f32[0][:], reads=[buf("ftile0")],
                         writes=[], note=f"tap_dn{t}a")
                    sdma(dd[:, 1024:2048], fhalf_f32[1][:], reads=[buf("ftile1")],
                         writes=[], note=f"tap_dn{t}b")
                    dd2 = tap_dram(f"gus{t}", [128, 2], F32)
                    sdma(dd2[:], gusum[:], reads=[buf("gusum")], writes=[],
                         note=f"tap_gus{t}")
                sb = ln_stats([(fhalf_f32[0][:], 1024), (fhalf_f32[1][:], 1024)],
                              [buf("ftile0"), buf("ftile1")], par, D, f"ln4t{t}")
                for half in range(2):
                    fb = buf(f"ftile{half}")
                    vop(lambda e, hf=half, p=par: e.tensor_scalar(
                        fhalf_f32[hf][:], fhalf_f32[hf][:],
                        nmean[:, p:p + 1], istd[:, p:p + 1], ALU.add, ALU.mult),
                        reads=[sb], writes=[fb], small=False,
                        note=f"ln4n{t}_{half}")
                    vop(lambda e, hf=half, t_=t: e.tensor_add(
                        fhalf_f32[hf][:], fhalf_f32[hf][:],
                        x1[:, t_, hf * 1024:(hf + 1) * 1024]),
                        reads=[buf("x1B")], writes=[fb], small=False,
                        note=f"res2_{t}_{half}")
                    sdma(y_d[t * 128:(t + 1) * 128, half * 1024:(half + 1) * 1024],
                         fhalf_f32[half][:], reads=[fb], writes=[fb],
                         note=f"out{t}_{half}")

    B.finalize()
    n_sp = sum(1 for o in B.ops if o.is_dma and o.eng == "sp")
    n_pl = sum(1 for o in B.ops if o.is_dma and o.eng == "pl")

    sems = {"pe": nc.alloc_semaphore("s_pe"), "dve": nc.alloc_semaphore("s_dve"),
            "act": nc.alloc_semaphore("s_act")}
    dma_sems = {"sp": nc.alloc_semaphore("d_sp"), "pl": nc.alloc_semaphore("d_pl")}

    with nc.Block() as block:
        @block.tensor
        def _(eng):
            B.emit_engine("pe", eng, sems, dma_sems)

        @block.vector
        def _(eng):
            B.emit_engine("dve", eng, sems, dma_sems)

        @block.scalar
        def _(eng):
            B.emit_engine("act", eng, sems, dma_sems)

        @block.sync
        def _(eng):
            B.emit_engine("sp", eng, sems, dma_sems)
            eng.wait_ge(dma_sems["sp"], 16 * n_sp)

        @block.gpsimd
        def _(eng):
            B.emit_engine("pl", eng, sems, dma_sems)
            eng.wait_ge(dma_sems["pl"], 16 * n_pl)

    return nc


def kernel(L0_attn_x, L0_attn_Wq, L0_attn_Wk, L0_attn_Wv, L0_attn_Wo,
           L0_ffn_wq_gate, L0_ffn_s_gate, L0_ffn_wq_up, L0_ffn_s_up,
           L0_ffn_wq_down, L0_ffn_s_down):
    x = np.ascontiguousarray(np.asarray(L0_attn_x, dtype=np.float32))
    packed = pack_inputs(L0_attn_Wq, L0_attn_Wk, L0_attn_Wv, L0_attn_Wo,
                         L0_ffn_wq_gate, L0_ffn_s_gate, L0_ffn_wq_up,
                         L0_ffn_s_up, L0_ffn_wq_down, L0_ffn_s_down)
    nc = build_program(DEBUG_TAPS)
    in_maps = []
    for c in range(8):
        m = {"x": np.ascontiguousarray(x[c])}
        m.update(packed)
        in_maps.append(m)
    res = run_bass_kernel_spmd(nc, in_maps, core_ids=list(range(8)))
    out = np.stack([np.asarray(res.results[c]["y"]) for c in range(8)], axis=0)
    return out
